# version 07
# baseline (speedup 1.0000x reference)
import numpy as np
import concourse.bass as bass
import concourse.mybir as mybir
from concourse.bass_utils import run_bass_kernel_spmd
from concourse.bass_core_rust import IndirectOffsetOnAxis

F32 = mybir.dt.float32
BF16 = mybir.dt.bfloat16
I32 = mybir.dt.int32
AF = mybir.ActivationFunctionType
ALU = mybir.AluOpType

DM = 2048
NDT = DM // 128
T = 4096
HF = 5632
NHT = HF // 128
HD = 128
NH = 16
NKV = 4
CH = 512
NCH = T // CH
NSLOT = 4
UPROF = [2, 4, 6, 8]
STRIPE = 2048
NSTRIPE = T // STRIPE
HC = 4
NHCH = NHT // HC
NWEL = HF * DM
EXP_SHIFT = -24.0
WASTE_BIAS = -100.0
MASKVAL = -10000.0
SCALE = float(np.float32(1.0) / np.sqrt(np.float32(HD)))
EVEN_CHUNKS = [0, 3, 4, 7]
ODD_CHUNKS = [1, 2, 5, 6]
OWN = NSLOT * CH
PAIRS = [[0, 1], [2, 3], [4, 5], [6, 7]]
CMAP = {}
for _s, _g in enumerate(EVEN_CHUNKS):
    CMAP[_g] = (0, _s)
for _s, _g in enumerate(ODD_CHUNKS):
    CMAP[_g] = (1, _s)


class Ch:
    def __init__(self, sem):
        self.sem = sem
        self.n = 0


def build_program(upto="full", dbg=()):
    nc = bass.Bass("TRN2", target_bir_lowering=False)
    dbg = set(dbg)

    xin = nc.dram_tensor("xin", [OWN, DM], F32, kind="ExternalInput").ap()
    w1 = nc.dram_tensor("w1", [HF, DM], F32, kind="ExternalInput").ap()
    w2 = nc.dram_tensor("w2", [HF, DM], F32, kind="ExternalInput").ap()
    w3 = nc.dram_tensor("w3", [DM, HF], F32, kind="ExternalInput").ap()
    wq = nc.dram_tensor("wq", [DM, DM], F32, kind="ExternalInput").ap()
    wk = nc.dram_tensor("wk", [DM, 512], F32, kind="ExternalInput").ap()
    wv = nc.dram_tensor("wv", [DM, 512], F32, kind="ExternalInput").ap()
    wo = nc.dram_tensor("wo", [DM, DM], F32, kind="ExternalInput").ap()
    gki = nc.dram_tensor("gki", [128, NSLOT * NKV], I32, kind="ExternalInput").ap()
    gvi = nc.dram_tensor("gvi", [128, NSLOT * 4], I32, kind="ExternalInput").ap()
    ebias = nc.dram_tensor("ebias", [128, NSLOT * 8], F32, kind="ExternalInput").ap()
    out = nc.dram_tensor("out", [2048, DM], F32, kind="ExternalOutput").ap()

    w1q_d = nc.dram_tensor("w1q_d", [HF, DM], BF16, kind="Internal").ap()
    w2q_d = nc.dram_tensor("w2q_d", [HF, DM], BF16, kind="Internal").ap()
    w3q_d = nc.dram_tensor("w3q_d", [DM, HF], BF16, kind="Internal").ap()
    w1T_d = nc.dram_tensor("w1T_d", [DM, HF], BF16, kind="Internal").ap()
    w2T_d = nc.dram_tensor("w2T_d", [DM, HF], BF16, kind="Internal").ap()
    w3T_d = nc.dram_tensor("w3T_d", [HF, DM], BF16, kind="Internal").ap()
    wqb_d = nc.dram_tensor("wqb_d", [DM, DM], BF16, kind="Internal").ap()
    wkb_d = nc.dram_tensor("wkb_d", [DM, 512], BF16, kind="Internal").ap()
    wvb_d = nc.dram_tensor("wvb_d", [DM, 512], BF16, kind="Internal").ap()
    wob_d = nc.dram_tensor("wob_d", [DM, DM], BF16, kind="Internal").ap()
    xbf_d = nc.dram_tensor("xbf_d", [OWN, DM], BF16, kind="Internal").ap()
    kk_part = nc.dram_tensor("kk_part", [NKV * 128, OWN], BF16, kind="Internal").ap()
    vv_part = nc.dram_tensor("vv_part", [OWN, 512], BF16, kind="Internal").ap()
    kk_all = nc.dram_tensor("kk_all", [2 * NKV * 128, OWN], BF16, kind="Internal").ap()
    vv_all = nc.dram_tensor("vv_all", [2 * OWN, 512], BF16, kind="Internal").ap()
    outT_d = nc.dram_tensor("outT_d", [DM, 2048], BF16, kind="Internal").ap()

    dbg_aps = {}
    if "gamma" in dbg:
        dbg_aps["dbg_gamma"] = nc.dram_tensor("dbg_gamma", [1, 8], F32, kind="ExternalOutput").ap()
    if "w1q" in dbg:
        dbg_aps["dbg_w1q"] = nc.dram_tensor("dbg_w1q", [HF, DM], BF16, kind="ExternalOutput").ap()
        dbg_aps["dbg_w2q"] = nc.dram_tensor("dbg_w2q", [HF, DM], BF16, kind="ExternalOutput").ap()
        dbg_aps["dbg_w3q"] = nc.dram_tensor("dbg_w3q", [DM, HF], BF16, kind="ExternalOutput").ap()
        dbg_aps["dbg_w1T"] = nc.dram_tensor("dbg_w1T", [DM, HF], BF16, kind="ExternalOutput").ap()
        dbg_aps["dbg_w2T"] = nc.dram_tensor("dbg_w2T", [DM, HF], BF16, kind="ExternalOutput").ap()
        dbg_aps["dbg_w3T"] = nc.dram_tensor("dbg_w3T", [HF, DM], BF16, kind="ExternalOutput").ap()
    if "kk" in dbg:
        dbg_aps["dbg_kk"] = nc.dram_tensor("dbg_kk", [2 * NKV * 128, OWN], BF16, kind="ExternalOutput").ap()
        dbg_aps["dbg_vv"] = nc.dram_tensor("dbg_vv", [2 * OWN, 512], BF16, kind="ExternalOutput").ap()
    if "acc" in dbg:
        dbg_aps["dbg_acc"] = nc.dram_tensor("dbg_acc", [DM, 2048], BF16, kind="ExternalOutput").ap()
    if "r1" in dbg:
        dbg_aps["dbg_r1"] = nc.dram_tensor("dbg_r1", [DM, 2048], BF16, kind="ExternalOutput").ap()

    arena = nc.alloc_sbuf_tensor("ARENA", [128, 200 * 1024], mybir.dt.uint8)
    ARENA_BASE = nc.lookup_mloc(arena).addr
    OFF_R1 = ARENA_BASE + 0
    OFF_R2 = ARENA_BASE + 64 * 1024
    OFF_R3 = ARENA_BASE + 128 * 1024
    OFF_P = ARENA_BASE + 192 * 1024

    def sb(name, shape, dtype, off):
        return nc.alloc_sbuf_tensor_at(name, shape, dtype, offset=off).ap()

    R1 = sb("R1", [128, NDT, 2048], BF16, OFF_R1)
    XH = sb("XH", [128, NDT, 2048], BF16, OFF_R2)
    x2o = sb("x2o", [128, NDT, 512], BF16, OFF_R2)
    atno = sb("atno", [128, NH, 512], BF16, OFF_R2)
    qh = sb("qh", [128, NH, 512], BF16, OFF_R2 + 16 * 1024)
    wobuf = sb("wobuf", [128, 2, NH, 128], BF16, OFF_R2 + 16 * 1024)
    probs = sb("probs", [128, 2, 2, 512], BF16, OFF_R2 + 32 * 1024)
    recipR = sb("recipR", [128, 512], F32, OFF_R2 + 40 * 1024)
    sums2 = sb("sums2", [1, 2, 512], F32, OFF_R2 + 42 * 1024)
    recip2 = sb("recip2", [1, 2, 512], F32, OFF_R2 + 43 * 1024)
    mean_s = sb("mean_s", [128, 512], F32, OFF_R2 + 44 * 1024)
    rstd_s = sb("rstd_s", [128, 512], F32, OFF_R2 + 46 * 1024)
    var_s = sb("var_s", [128, 512], F32, OFF_R2 + 48 * 1024)
    wqbuf = sb("wqbuf", [128, 2, NDT, 128], BF16, OFF_R2 + 50 * 1024)
    hid = sb("hid", [128, HC, 2048], BF16, OFF_R3)
    w3buf = sb("w3buf", [128, HC, 2048], BF16, OFF_R3 + 16 * 1024)
    w1buf = sb("w1buf", [128, 2, NDT, 128], BF16, OFF_R3 + 32 * 1024)
    w2buf = sb("w2buf", [128, 2, NDT, 128], BF16, OFF_R3 + 40 * 1024)
    sgbuf = sb("sgbuf", [128, 2, 512], BF16, OFF_R3 + 48 * 1024)
    gwtile = sb("gwtile", [128, 3, 2048], F32, OFF_R3)
    qtile = sb("qtile", [128, 3, 2048], BF16, OFF_R3 + 24 * 1024)
    q2tile = sb("q2tile", [128, 2048], BF16, OFF_R3 + 36 * 1024)
    tbuf = sb("tbuf", [128, 2, 5632], BF16, OFF_R3 + 40 * 1024)
    kstag = sb("kstag", [128, NKV, 2048], BF16, OFF_R3)
    vstag = sb("vstag", [128, 16, 512], BF16, OFF_R3 + 16 * 1024)
    wkbuf = sb("wkbuf", [128, NDT, 512], BF16, OFF_R3 + 32 * 1024)
    wvbuf = sb("wvbuf", [128, NDT, 512], BF16, OFF_R3 + 48 * 1024)
    mean_rep = sb("mean_rep", [128, 2048], F32, OFF_R3)
    rstd_rep = sb("rstd_rep", [128, 2048], F32, OFF_R3 + 8 * 1024)
    var_tmp = sb("var_tmp", [128, 2048], F32, OFF_R3 + 16 * 1024)
    kch = sb("kch", [128, 7, NKV, 512], BF16, OFF_R3)
    vch = sb("vch", [128, 7, 4, NKV, 128], BF16, OFF_R3 + 28 * 1024)
    kdb = sb("kdb", [128, NKV, 512], BF16, OFF_R3 + 56 * 1024)
    vdb = sb("vdb", [128, 4, NKV, 128], BF16, OFF_R3 + 60 * 1024)
    M_diag = sb("M_diag", [128, 4, 512], BF16, OFF_P)
    ones128 = sb("ones128", [128, 128], BF16, OFF_P + 4 * 1024)
    onescol = sb("onescol", [128, 1], BF16, OFF_P + 4 * 1024 + 256)
    onesf_col = sb("onesf_col", [128, 1], F32, OFF_P + 4 * 1024 + 512)
    onesf_row = sb("onesf_row", [1, 128], F32, OFF_P + 4 * 1024 + 1024)
    gki_s = sb("gki_s", [128, NSLOT * NKV], I32, OFF_P + 6 * 1024 + 256)
    gvi_s = sb("gvi_s", [128, NSLOT * 4], I32, OFF_P + 6 * 1024 + 512)
    ebias_s = sb("ebias_s", [128, NSLOT * 8], F32, OFF_P + 6 * 1024 + 768)
    thr_s = sb("thr_s", [128, 8], F32, OFF_P + 7 * 1024)
    colsums = sb("colsums", [128, 3 * NHT], F32, OFF_P + 7 * 1024 + 64)
    gtmp = sb("gtmp", [1, 8], F32, OFF_P + 7 * 1024 + 640)
    gtmp2 = sb("gtmp2", [1, 8], F32, OFF_P + 7 * 1024 + 704)
    cvals = sb("cvals", [128, 4], F32, OFF_P + 7 * 1024 + 768)
    iotmp = sb("iotmp", [128, 4, 512], I32, OFF_R3 + 56 * 1024)
    tbw3 = sb("tbw3", [128, 2, 2048], BF16, OFF_R3 + 56 * 1024)

    psA = nc.alloc_psum_tensor("psA", [128, 2048], F32).ap()
    psB = nc.alloc_psum_tensor("psB", [128, 2048], F32).ap()

    GP = Ch(nc.alloc_semaphore(name="GP"))
    WS = Ch(nc.alloc_semaphore(name="WS"))
    WS2 = Ch(nc.alloc_semaphore(name="WS2"))
    MM = Ch(nc.alloc_semaphore(name="MM"))
    ACT = Ch(nc.alloc_semaphore(name="ACT"))
    VEC = Ch(nc.alloc_semaphore(name="VEC"))
    ST = Ch(nc.alloc_semaphore(name="ST"))
    AGS = Ch(nc.alloc_semaphore(name="AGS"))
    QL = [[Ch(nc.alloc_semaphore(name=f"QL{m}{p}")) for p in range(2)] for m in range(3)]
    QS = [Ch(nc.alloc_semaphore(name=f"QS{m}")) for m in range(3)]
    XBF = Ch(nc.alloc_semaphore(name="XBF"))
    XTP = Ch(nc.alloc_semaphore(name="XTP"))
    W3T = Ch(nc.alloc_semaphore(name="W3T"))
    W3TT = Ch(nc.alloc_semaphore(name="W3TT"))

    ENG = {"g": [], "sy": [], "t": [], "s": [], "v": []}

    def op(engine, fn, inc=None, k=1):
        if inc is not None:
            inc.n += k
            ENG[engine].append(lambda e, fn=fn, s=inc.sem, k=k: fn(e).then_inc(s, k))
            return inc.n
        ENG[engine].append(fn)
        return None

    def wait(engine, ch, val=None):
        v = ch.n if val is None else val
        if v > 0:
            ENG[engine].append(lambda e, s=ch.sem, v=v: e.wait_ge(s, v))

    def barrier():
        for e in ENG:
            for ch in (GP, WS, WS2, MM, ACT, VEC, AGS, XTP, W3T, W3TT):
                wait(e, ch)

    def ln_stats(src_tiles, sq_tiles, ncols, mean_ap, rstd_ap, var_ap):
        nnc = ncols // 512
        vsq = [0] * NDT
        for d in range(NDT):
            vsq[d] = op("v", lambda v, d=d: v.tensor_mul(sq_tiles[d], src_tiles[d],
                                                         src_tiles[d]), inc=VEC)
        for d in range(NDT):
            for j in range(nnc):
                op("t", lambda t, d=d, j=j: t.matmul(
                    psA[:, j * 512:(j + 1) * 512], ones128[:],
                    src_tiles[d][:, j * 512:(j + 1) * 512],
                    start=(d == 0), stop=(d == NDT - 1)), inc=None)
        for d in range(NDT):
            wait("t", VEC, vsq[d])
            for j in range(nnc):
                op("t", lambda t, d=d, j=j: t.matmul(
                    psB[:, j * 512:(j + 1) * 512], ones128[:],
                    sq_tiles[d][:, j * 512:(j + 1) * 512],
                    start=(d == 0), stop=(d == NDT - 1)),
                    inc=MM if (d == NDT - 1 and j == nnc - 1) else None)
        wait("v", MM)
        op("v", lambda v: v.tensor_scalar(mean_ap[:, 0:ncols], psA[:, 0:ncols],
                                          1.0 / DM, None, ALU.mult), inc=VEC)
        op("v", lambda v: v.tensor_scalar(var_ap[:, 0:ncols], psB[:, 0:ncols],
                                          1.0 / DM, None, ALU.mult), inc=VEC)
        op("v", lambda v: v.tensor_mul(rstd_ap[:, 0:ncols], mean_ap[:, 0:ncols],
                                       mean_ap[:, 0:ncols]), inc=VEC)
        op("v", lambda v: v.tensor_sub(var_ap[:, 0:ncols], var_ap[:, 0:ncols],
                                       rstd_ap[:, 0:ncols]), inc=VEC)
        wait("s", VEC)
        op("s", lambda s: s.activation(rstd_ap[:, 0:ncols], var_ap[:, 0:ncols],
                                       AF.Ln, bias=cvals[:, 0:1], scale=1.0), inc=ACT)
        op("s", lambda s: s.activation(rstd_ap[:, 0:ncols], rstd_ap[:, 0:ncols],
                                       AF.Exp, bias=0.0, scale=cvals[:, 1:2]), inc=ACT)
        wait("v", ACT)

    def ln_apply(src_tiles, dst_tiles, ncols, mean_ap, rstd_ap, tmp_ap):
        for d in range(NDT):
            op("v", lambda v, d=d: v.tensor_sub(tmp_ap[:, 0:ncols], src_tiles[d],
                                                mean_ap[:, 0:ncols]), inc=VEC)
            op("v", lambda v, d=d: v.tensor_mul(dst_tiles[d], tmp_ap[:, 0:ncols],
                                                rstd_ap[:, 0:ncols]), inc=VEC)

    def ffn_pass(ntok, first=False, w3t_gate=None, pre_gate=None, tp_hook=None):
        ntt = ntok // 512
        if first:
            wait("t", VEC)
        for hc_i in range(NHCH):
            wait("s", MM)
            if w3t_gate is not None:
                wait("s", W3T, w3t_gate[hc_i])
            for hti in range(HC):
                ht = hc_i * HC + hti
                op("s", lambda s, ht=ht, hti=hti: s.dma_start(
                    out=w3buf[:, hti, :],
                    in_=w3T_d[ht * 128:(ht + 1) * 128, :]), inc=WS2, k=16)
            for hti in range(HC):
                ht = hc_i * HC + hti
                pp = ht % 2
                if ht >= 2:
                    wait("sy", MM)
                if first and ht == 0:
                    wait("sy", WS2, pre_gate[0])
                    wait("sy", QS[2], pre_gate[1])
                op("sy", lambda sy, ht=ht, pp=pp: sy.dma_start(
                    out=w1buf[:, pp, :, :],
                    in_=w1T_d.rearrange("(dt p) H -> p dt H", p=128)[:, :, ht * 128:(ht + 1) * 128]),
                    inc=WS, k=16)
                op("sy", lambda sy, ht=ht, pp=pp: sy.dma_start(
                    out=w2buf[:, pp, :, :],
                    in_=w2T_d.rearrange("(dt p) H -> p dt H", p=128)[:, :, ht * 128:(ht + 1) * 128]),
                    inc=WS, k=16)
                wait("t", WS)
                for tt in range(ntt):
                    sl = tt % 2
                    x1ap = psA[:, sl * 512:(sl + 1) * 512]
                    x2ap = psA[:, 1024 + sl * 512:1024 + (sl + 1) * 512]
                    wait("t", VEC, VEC.n - 1)
                    for d in range(NDT):
                        op("t", lambda t, d=d, pp=pp, tt=tt, x1ap=x1ap: t.matmul(
                            x1ap, w1buf[:, pp, d, :], XH[:, d, tt * 512:(tt + 1) * 512],
                            start=(d == 0), stop=(d == NDT - 1)),
                            inc=MM if d == NDT - 1 else None)
                    for d in range(NDT):
                        op("t", lambda t, d=d, pp=pp, tt=tt, x2ap=x2ap: t.matmul(
                            x2ap, w2buf[:, pp, d, :], XH[:, d, tt * 512:(tt + 1) * 512],
                            start=(d == 0), stop=(d == NDT - 1)),
                            inc=MM if d == NDT - 1 else None)
                    wait("s", MM, MM.n - 1)
                    op("s", lambda s, sl=sl, x1ap=x1ap: s.activation(
                        sgbuf[:, sl, :], x1ap, AF.Sigmoid), inc=ACT)
                    wait("v", ACT)
                    wait("v", MM)
                    op("v", lambda v, sl=sl, hti=hti, tt=tt, x2ap=x2ap: v.tensor_mul(
                        hid[:, hti, tt * 512:(tt + 1) * 512], sgbuf[:, sl, :], x2ap), inc=VEC)
            if tp_hook is not None:
                tp_hook(hc_i + 1)
            wait("t", WS2)
            wait("t", VEC)
            for d in range(NDT):
                for tt in range(ntt):
                    sl = (d * ntt + tt) % 2
                    oap = psB[:, sl * 512:(sl + 1) * 512]
                    wait("t", VEC, VEC.n - 1)
                    for hti in range(HC):
                        op("t", lambda t, d=d, tt=tt, hti=hti, oap=oap: t.matmul(
                            oap, w3buf[:, hti, d * 128:(d + 1) * 128],
                            hid[:, hti, tt * 512:(tt + 1) * 512],
                            start=(hti == 0), stop=(hti == HC - 1)),
                            inc=MM if hti == HC - 1 else None)
                    wait("v", MM)
                    op("v", lambda v, d=d, tt=tt, oap=oap: v.tensor_add(
                        R1[:, d, tt * 512:(tt + 1) * 512],
                        R1[:, d, tt * 512:(tt + 1) * 512], oap), inc=VEC)

    op("g", lambda g: g.memset(ones128[:], 1.0))
    op("g", lambda g: g.memset(cvals[:, 0:1], 1e-5))
    op("g", lambda g: g.memset(cvals[:, 1:2], -0.5))
    op("g", lambda g: g.memset(cvals[:, 2:3], SCALE))
    op("g", lambda g: g.memset(onescol[:], 1.0))
    op("g", lambda g: g.memset(onesf_col[:], 1.0))
    op("g", lambda g: g.memset(onesf_row[:], 1.0))
    op("g", lambda g: g.iota(iotmp[:], pattern=[[128, 4], [-1, 512]], base=0,
                             channel_multiplier=1), inc=GP, k=1)
    op("g", lambda g: g.dma_start(out=gki_s[:], in_=gki[:]), inc=GP, k=16)
    op("g", lambda g: g.dma_start(out=gvi_s[:], in_=gvi[:]), inc=GP, k=16)
    op("g", lambda g: g.dma_start(out=ebias_s[:], in_=ebias[:]), inc=GP, k=16)
    op("g", lambda g: g.dma_start(out=xbf_d[:], in_=xin[:]), inc=XBF, k=16)
    op("g", lambda g: g.dma_start(out=wqb_d[:], in_=wq[:]), inc=GP, k=16)
    op("g", lambda g: g.dma_start(out=wkb_d[:], in_=wk[:]), inc=GP, k=16)
    op("g", lambda g: g.dma_start(out=wvb_d[:], in_=wv[:]), inc=GP, k=16)
    op("g", lambda g: g.dma_start(out=wob_d[:], in_=wo[:]), inc=GP, k=16)
    gp_wcast = GP.n
    wait("v", GP, 1)
    op("v", lambda v: v.tensor_scalar(M_diag[:], iotmp[:], 0.0, MASKVAL,
                                      ALU.is_gt, ALU.mult), inc=VEC)

    WMATS = [
        (w1, w1q_d, False, 0),
        (w2, w2q_d, False, 2),
        (w3, w3q_d, True, 4),
    ]
    QENG = [("sy", WS), ("s", WS2), ("g", GP)]
    NTILE = NWEL // (128 * 2048)
    flats = []
    for (wsrc, qdst, fold, ti) in WMATS:
        flats.append((
            wsrc.rearrange("a b -> (a b)").rearrange("(t p c) -> t p c", p=128, c=2048),
            qdst.rearrange("a b -> (a b)").rearrange("(t p c) -> t p c", p=128, c=2048)))
    red_cnt = [[0] * NTILE for _ in range(3)]
    load_cnt = [[0] * NTILE for _ in range(3)]
    for t_i in range(NTILE):
        for mi in range(3):
            eng, _ = QENG[mi]
            if t_i > 0:
                wait(eng, VEC, red_cnt[mi][t_i - 1])
            load_cnt[mi][t_i] = op(eng, lambda e, mi=mi, t_i=t_i:
                                   e.dma_start(out=gwtile[:, mi, :], in_=flats[mi][0][t_i]),
                                   inc=QL[mi][t_i % 2], k=16)
        for mi in range(3):
            wait("v", QL[mi][t_i % 2], load_cnt[mi][t_i])
            red_cnt[mi][t_i] = op("v", lambda v, mi=mi, t_i=t_i:
                                  v.tensor_reduce(colsums[:, mi * NTILE + t_i:mi * NTILE + t_i + 1],
                                                  gwtile[:, mi, :], mybir.AxisListType.X, ALU.add,
                                                  apply_absolute_value=True), inc=VEC)
    wait("s", XBF, 16)
    for d in range(NDT):
        op("s", lambda s, d=d: s.dma_start_transpose(
            R1[:, d, :], xbf_d[0:OWN, d * 128:(d + 1) * 128]), inc=XTP, k=16)
    for mi, (wsrc, qdst, fold, ti) in enumerate(WMATS):
        wait("t", VEC)
        op("t", lambda t, mi=mi: t.matmul(psA[0:1, mi * 64:mi * 64 + NTILE], onesf_col[:],
                                          colsums[:, mi * NTILE:(mi + 1) * NTILE],
                                          start=True, stop=True), inc=MM)
        wait("v", MM)
        op("v", lambda v, mi=mi, ti=ti: v.tensor_reduce(
            gtmp[0:1, ti:ti + 1], psA[0:1, mi * 64:mi * 64 + NTILE],
            mybir.AxisListType.X, ALU.add), inc=VEC)
        wait("v", VEC)
        op("v", lambda v, ti=ti: v.tensor_scalar(
            gtmp[0:1, ti:ti + 1], gtmp[0:1, ti:ti + 1],
            float(np.float32(1.0) / np.float32(NWEL)), None, ALU.mult), inc=VEC)
        wait("v", VEC)
        op("v", lambda v, ti=ti: v.tensor_scalar(
            gtmp2[0:1, ti:ti + 1], gtmp[0:1, ti:ti + 1],
            1e-8, 0.5, ALU.add, ALU.mult), inc=VEC)
        wait("v", VEC)
        op("v", lambda v, ti=ti: v.tensor_scalar(
            gtmp2[0:1, ti + 1:ti + 2], gtmp2[0:1, ti:ti + 1],
            -1.0, None, ALU.mult), inc=VEC)
        wait("t", VEC)
        wait("t", ACT)
        op("t", lambda t, mi=mi, ti=ti: t.matmul(psB[:, mi * 2:mi * 2 + 2], onesf_row[:],
                                                 gtmp2[0:1, ti:ti + 2], start=True, stop=True), inc=MM)
        wait("s", MM)
        op("s", lambda s, mi=mi, ti=ti: s.copy(thr_s[:, ti:ti + 2], psB[:, mi * 2:mi * 2 + 2]), inc=ACT)
    act_thr = ACT.n
    qnt_cnt = [[0] * NTILE for _ in range(3)]
    st_cnt = [[0] * NTILE for _ in range(3)]
    for t_i in range(NTILE):
        for mi in range(3):
            eng, _ = QENG[mi]
            wait(eng, VEC, (red_cnt[mi][NTILE - 1] if t_i == 0 else qnt_cnt[mi][t_i - 1]))
            load_cnt[mi][t_i] = op(eng, lambda e, mi=mi, t_i=t_i:
                                   e.dma_start(out=gwtile[:, mi, :], in_=flats[mi][0][t_i]),
                                   inc=QL[mi][t_i % 2], k=16)
        for mi, (wsrc, qdst, fold, ti) in enumerate(WMATS):
            phl = 0.5 if fold else 1.0
            wait("v", QL[mi][t_i % 2], load_cnt[mi][t_i])
            if t_i == 0:
                wait("v", ACT, act_thr)
            if t_i >= 1:
                wait("v", QS[mi], st_cnt[mi][t_i - 1])

            def quant(v, mi=mi, ti=ti, phl=phl):
                v.tensor_scalar(qtile[:, mi, :], gwtile[:, mi, :],
                                thr_s[:, ti:ti + 1], phl, ALU.is_ge, ALU.mult)
                v.tensor_scalar(q2tile[:], gwtile[:, mi, :],
                                thr_s[:, ti + 1:ti + 2], phl, ALU.is_le, ALU.mult)
                return v.tensor_sub(qtile[:, mi, :], qtile[:, mi, :], q2tile[:])
            qnt_cnt[mi][t_i] = op("v", quant, inc=VEC)
            wait("g", VEC, qnt_cnt[mi][t_i])
            st_cnt[mi][t_i] = op("g", lambda g, mi=mi, t_i=t_i:
                                 g.dma_start(out=flats[mi][1][t_i], in_=qtile[:, mi, :]),
                                 inc=QS[mi], k=16)

    r1_tiles = [R1[:, d, :] for d in range(NDT)]
    xh_tiles = [XH[:, d, :] for d in range(NDT)]
    wait("v", XTP)
    wait("t", XTP)
    wait("t", VEC)
    wait("t", ACT, act_thr)
    ln_stats(r1_tiles, xh_tiles, OWN, mean_rep, rstd_rep, var_tmp)
    ln_apply(r1_tiles, xh_tiles, OWN, mean_rep, rstd_rep, var_tmp)

    TMATS = [(w1q_d, w1T_d, HF, DM, 0), (w2q_d, w2T_d, HF, DM, 1)]
    for (src_d, dst_d, rows, cols, mi) in TMATS:
        wait("sy", QS[mi], st_cnt[mi][NTILE - 1])
        for d in range(cols // 128):
            pp = d % 2
            wait("sy", WS2, WS2.n if d < 2 else WS2.n - 16)
            op("sy", lambda sy, d=d, pp=pp, src_d=src_d, rows=rows:
               sy.dma_start_transpose(tbuf[:, pp, 0:rows],
                                      src_d[:, d * 128:(d + 1) * 128]), inc=WS, k=16)
            wait("s", WS)
            op("s", lambda s, d=d, pp=pp, dst_d=dst_d, rows=rows:
               s.dma_start(out=dst_d[d * 128:(d + 1) * 128, :],
                           in_=tbuf[:, pp, 0:rows]), inc=WS2, k=16)
    w1w2t_done = WS2.n
    w3q_done = st_cnt[2][NTILE - 1]
    w3t_st = [0] * NHT
    w3t_chunk = [0] * NHCH

    def emit_w3_chunk(hc):
        if hc >= NHCH:
            return
        for d in range(hc * HC, (hc + 1) * HC):
            if d == 0:
                wait("sy", QS[2], w3q_done)
                wait("sy", WS2, w1w2t_done)
            if d >= 2:
                wait("sy", W3T, w3t_st[d - 2])
            op("sy", lambda sy, d=d: sy.dma_start_transpose(
                tbw3[:, d % 2, :], w3q_d[:, d * 128:(d + 1) * 128]), inc=W3TT, k=16)
            wait("g", W3TT)
            w3t_st[d] = op("g", lambda g, d=d: g.dma_start(
                out=w3T_d[d * 128:(d + 1) * 128, :], in_=tbw3[:, d % 2, :]), inc=W3T, k=16)
        w3t_chunk[hc] = w3t_st[(hc + 1) * HC - 1]

    emit_w3_chunk(0)

    if "gamma" in dbg:
        wait("s", VEC)
        op("s", lambda s: s.dma_start(out=dbg_aps["dbg_gamma"][:], in_=gtmp[0:1, :]), inc=ST, k=16)
    if "w1q" in dbg:
        wait("g", WS2)
        wait("g", GP)
        wait("g", W3T)
        op("g", lambda g: g.dma_start(out=dbg_aps["dbg_w1q"][:], in_=w1q_d[:]), inc=ST, k=16)
        op("g", lambda g: g.dma_start(out=dbg_aps["dbg_w2q"][:], in_=w2q_d[:]), inc=ST, k=16)
        op("g", lambda g: g.dma_start(out=dbg_aps["dbg_w3q"][:], in_=w3q_d[:]), inc=ST, k=16)
        op("g", lambda g: g.dma_start(out=dbg_aps["dbg_w1T"][:], in_=w1T_d[:]), inc=ST, k=16)
        op("g", lambda g: g.dma_start(out=dbg_aps["dbg_w2T"][:], in_=w2T_d[:]), inc=ST, k=16)
        op("g", lambda g: g.dma_start(out=dbg_aps["dbg_w3T"][:], in_=w3T_d[:]), inc=ST, k=16)

    if upto == "prep":
        _finish(nc, ENG, ST)
        return nc, dbg_aps

    ffn_pass(OWN, first=True, w3t_gate=w3t_chunk, pre_gate=(w1w2t_done, w3q_done),
             tp_hook=emit_w3_chunk)
    barrier()
    if "r1" in dbg:
        for d in range(NDT):
            op("g", lambda g, d=d: g.dma_start(
                out=dbg_aps["dbg_r1"][d * 128:(d + 1) * 128, :], in_=R1[:, d, :]), inc=ST, k=16)
        wait("v", ST)
    ln_stats(r1_tiles, xh_tiles, OWN, mean_rep, rstd_rep, var_tmp)
    ln_apply(r1_tiles, xh_tiles, OWN, mean_rep, rstd_rep, var_tmp)
    barrier()
    wait("sy", GP, gp_wcast)
    op("sy", lambda sy: sy.dma_start(
        out=wkbuf[:], in_=wkb_d.rearrange("(dt p) f -> p dt f", p=128)), inc=WS, k=16)
    op("sy", lambda sy: sy.dma_start(
        out=wvbuf[:], in_=wvb_d.rearrange("(dt p) f -> p dt f", p=128)), inc=WS, k=16)
    wait("t", WS)
    for kvh in range(NKV):
        for tt in range(OWN // 512):
            sl = (kvh * 4 + tt) % 2
            pap = psA[:, sl * 512:(sl + 1) * 512]
            wait("t", ACT, ACT.n - 1)
            for d in range(NDT):
                op("t", lambda t, d=d, kvh=kvh, tt=tt, pap=pap: t.matmul(
                    pap, wkbuf[:, d, kvh * 128:(kvh + 1) * 128],
                    XH[:, d, tt * 512:(tt + 1) * 512],
                    start=(d == 0), stop=(d == NDT - 1)), inc=MM if d == NDT - 1 else None)
            wait("s", MM)
            op("s", lambda s, kvh=kvh, tt=tt, pap=pap: s.copy(
                kstag[:, kvh, tt * 512:(tt + 1) * 512], pap), inc=ACT)
    for tk in range(OWN // 128):
        sl = tk % 2
        pap = psB[:, sl * 512:(sl + 1) * 512]
        wait("t", ACT, ACT.n - 1)
        for d in range(NDT):
            op("t", lambda t, d=d, tk=tk, pap=pap: t.matmul(
                pap, XH[:, d, tk * 128:(tk + 1) * 128], wvbuf[:, d, :],
                start=(d == 0), stop=(d == NDT - 1)), inc=MM if d == NDT - 1 else None)
        wait("s", MM)
        op("s", lambda s, tk=tk, pap=pap: s.copy(vstag[:, tk, :], pap), inc=ACT)
    wait("g", ACT)
    op("g", lambda g: g.dma_start(
        out=kk_part.rearrange("(k p) t -> p k t", p=128), in_=kstag[:]), inc=GP, k=16)
    op("g", lambda g: g.dma_start(
        out=vv_part.rearrange("(tk p) f -> p tk f", p=128), in_=vstag[:]), inc=GP, k=16)
    wait("g", GP)
    op("g", lambda g: g.collective_compute(
        "AllGather", ALU.bypass, replica_groups=PAIRS,
        ins=[kk_part.opt()], outs=[kk_all.opt()]), inc=AGS, k=1)
    op("g", lambda g: g.collective_compute(
        "AllGather", ALU.bypass, replica_groups=PAIRS,
        ins=[vv_part.opt()], outs=[vv_all.opt()]), inc=AGS, k=1)

    if "kk" in dbg:
        wait("g", AGS)
        op("g", lambda g: g.dma_start(out=dbg_aps["dbg_kk"][:], in_=kk_all[:]), inc=ST, k=16)
        op("g", lambda g: g.dma_start(out=dbg_aps["dbg_vv"][:], in_=vv_all[:]), inc=ST, k=16)

    if upto == "p1":
        _finish(nc, ENG, ST)
        return nc, dbg_aps

    kk_rows = kk_all.rearrange("r (s t) -> (r s) t", t=512)
    loaded_chunks = [0]

    def load_chunks(upper):
        for c in range(loaded_chunks[0], upper):
            cr, cp = CMAP[c]
            op("sy", lambda sy, c=c, cr=cr, cp=cp: sy.dma_start(
                out=kch[:, c, :, :],
                in_=kk_all[cr * 512:(cr + 1) * 512,
                           cp * 512:(cp + 1) * 512].rearrange("(k p) t -> p k t", p=128)),
                inc=WS, k=16)
            op("sy", lambda sy, c=c, cr=cr, cp=cp: sy.dma_start(
                out=vch[:, c, :, :, :],
                in_=vv_all[cr * OWN + cp * 512:cr * OWN + (cp + 1) * 512,
                           :].rearrange("(j p) (k d) -> p j k d", p=128, d=128)),
                inc=WS, k=16)
        loaded_chunks[0] = max(loaded_chunks[0], upper)

    for s_i in range(NSLOT):
        U = UPROF[s_i]
        if s_i == 0:
            wait("g", AGS)
            wait("sy", AGS)
        for kvh in range(NKV):
            op("g", lambda g, kvh=kvh, s_i=s_i: g.indirect_dma_start(
                kdb[:, kvh, :], None, kk_rows,
                IndirectOffsetOnAxis(gki_s[:, s_i * NKV + kvh:s_i * NKV + kvh + 1], 0)),
                inc=GP, k=16)
        for j in range(4):
            op("g", lambda g, j=j, s_i=s_i: g.indirect_dma_start(
                vdb[:, j, :, :], None, vv_all,
                IndirectOffsetOnAxis(gvi_s[:, s_i * 4 + j:s_i * 4 + j + 1], 0)),
                inc=GP, k=16)
        load_chunks(U - 1)
        wait("t", GP)
        wait("v", GP)
        sl_tiles = [R1[:, d, s_i * 512:(s_i + 1) * 512] for d in range(NDT)]
        xo_tiles = [x2o[:, d, :] for d in range(NDT)]
        ln_stats(sl_tiles, xo_tiles, 512, mean_s, rstd_s, var_s)
        ln_apply(sl_tiles, xo_tiles, 512, mean_s, rstd_s, var_s)
        barrier()
        for hh in range(NH):
            pp = hh % 2
            if hh >= 2:
                wait("sy", MM)
            op("sy", lambda sy, hh=hh, pp=pp: sy.dma_start(
                out=wqbuf[:, pp, :, :],
                in_=wqb_d.rearrange("(dt p) f -> p dt f", p=128)[:, :, hh * 128:(hh + 1) * 128]),
                inc=WS, k=16)
            wait("t", WS)
            pap = psB[:, (hh % 2) * 512:((hh % 2) + 1) * 512]
            wait("t", ACT, ACT.n - 1)
            for d in range(NDT):
                op("t", lambda t, d=d, pp=pp, pap=pap: t.matmul(
                    pap, wqbuf[:, pp, d, :], x2o[:, d, :],
                    start=(d == 0), stop=(d == NDT - 1)), inc=MM if d == NDT - 1 else None)
            wait("s", MM)
            op("s", lambda s, hh=hh, pap=pap: s.copy(qh[:, hh, :], pap), inc=ACT)
        barrier()
        if s_i + 1 < NSLOT:
            load_chunks(UPROF[s_i + 1] - 1)
        act_evac = {}
        vec_mul = {-1: VEC.n, 0: VEC.n}
        sums_copy = {}
        recip_n = {}
        pend = None
        for h in range(NH):
            kvh = h // NKV
            osl = h % 2
            psO = psB[:, osl * 512:(osl + 1) * 512]
            psS = psB[0:1, 1024 + osl * 512:1024 + osl * 512 + 512]
            if h >= 2:
                wait("t", ACT, act_evac[h - 2])
                wait("t", VEC, sums_copy[h - 2])
            if h >= 1:
                wait("t", VEC, vec_mul.get(h - 2, 0))
            sus = [(u, su) for u in range(U) for su in range(2)]
            nsu = len(sus)

            def emit_qk(i):
                u, su = sus[i]
                kbuf = kdb if u == U - 1 else kch[:, u, :, :]
                scA = psA[:, (i % 2) * 1024:(i % 2) * 1024 + 1024]
                wait("t", ACT, ACT.n - 1)
                for jj in range(2):
                    jg = su * 2 + jj
                    op("t", lambda t, jg=jg, jj=jj, kbuf=kbuf, kvh=kvh, h=h, scA=scA: t.matmul(
                        scA[:, jj * 512:(jj + 1) * 512],
                        kbuf[:, kvh, jg * 128:(jg + 1) * 128], qh[:, h, :],
                        start=True, stop=True), inc=MM if jj == 1 else None)

            emit_qk(0)
            for i in range(nsu):
                u, su = sus[i]
                is_diag = (u == U - 1)
                scA = psA[:, (i % 2) * 1024:(i % 2) * 1024 + 1024]
                pb = i % 2
                if is_diag:
                    wait("v", MM)
                    op("v", lambda v, su=su, scA=scA: v.scalar_tensor_tensor(
                        scA[:],
                        M_diag[:, su * 2:(su + 1) * 2, :].rearrange("p a b -> p (a b)"),
                        1.0, scA[:], ALU.mult, ALU.add), inc=VEC)
                    wait("s", VEC)
                else:
                    wait("s", MM)
                op("s", lambda s, pb=pb, s_i=s_i, u=u, scA=scA: s.activation(
                    probs[:, pb, :, :].rearrange("p a b -> p (a b)"), scA[:],
                    AF.Exp, bias=ebias_s[:, s_i * 8 + u:s_i * 8 + u + 1], scale=cvals[:, 2:3]),
                    inc=ACT)
                if i + 1 < nsu:
                    emit_qk(i + 1)
                if i == 1 and pend is not None:
                    hp = pend
                    bb = psB[:, (hp % 2) * 512:((hp % 2) + 1) * 512]
                    wait("t", ACT, act_evac[hp])
                    wait("t", VEC, recip_n[hp])
                    op("t", lambda t, hp=hp, bb=bb: t.matmul(
                        bb, onesf_row[:], recip2[0:1, hp % 2, :],
                        start=True, stop=True), inc=MM)
                    wait("v", MM)
                    vec_mul[hp] = op("v", lambda v, hp=hp, bb=bb: v.tensor_mul(
                        atno[:, hp, :], bb, atno[:, hp, :]), inc=VEC)
                    pend = None
                wait("t", ACT)
                vbuf = vdb if is_diag else vch[:, u, :, :, :]
                last = (i == nsu - 1)
                for jj in range(2):
                    jg = su * 2 + jj
                    op("t", lambda t, jj=jj, pb=pb, psS=psS,
                       first=(i == 0 and jj == 0), stp=(last and jj == 1): t.matmul(
                        psS, onescol[:], probs[:, pb, jj, :],
                        start=first, stop=stp), inc=None)
                    op("t", lambda t, jj=jj, jg=jg, pb=pb, vbuf=vbuf, kvh=kvh,
                       first=(i == 0 and jj == 0), stp=(last and jj == 1), psO=psO: t.matmul(
                        psO, vbuf[:, jg, kvh, :], probs[:, pb, jj, :],
                        start=first, stop=stp),
                        inc=MM if jj == 1 else None)
            wait("s", MM)
            act_evac[h] = op("s", lambda s, h=h, psO=psO: s.copy(atno[:, h, :], psO), inc=ACT)
            wait("v", MM)
            sums_copy[h] = op("v", lambda v, osl=osl, psS=psS: v.tensor_copy(
                sums2[0:1, osl, :], psS), inc=VEC)
            wait("v", VEC)
            recip_n[h] = op("v", lambda v, osl=osl: v.reciprocal(
                recip2[0:1, osl, :], sums2[0:1, osl, :]), inc=VEC)
            pend = h
        hp = pend
        bb = psB[:, (hp % 2) * 512:((hp % 2) + 1) * 512]
        wait("t", ACT, act_evac[hp])
        wait("t", VEC, recip_n[hp])
        op("t", lambda t, hp=hp, bb=bb: t.matmul(
            bb, onesf_row[:], recip2[0:1, hp % 2, :], start=True, stop=True), inc=MM)
        wait("v", MM)
        op("v", lambda v, hp=hp, bb=bb: v.tensor_mul(
            atno[:, hp, :], bb, atno[:, hp, :]), inc=VEC)
        barrier()
        for d in range(NDT):
            pp = d % 2
            if d >= 2:
                wait("sy", MM)
            op("sy", lambda sy, d=d, pp=pp: sy.dma_start(
                out=wobuf[:, pp, :, :],
                in_=wob_d.rearrange("(hb p) f -> p hb f", p=128)[:, :, d * 128:(d + 1) * 128]),
                inc=WS, k=16)
            wait("t", WS)
            pap = psA[:, (d % 2) * 512:((d % 2) + 1) * 512]
            wait("t", VEC, VEC.n - 1)
            for hb in range(NH):
                op("t", lambda t, hb=hb, pp=pp, pap=pap: t.matmul(
                    pap, wobuf[:, pp, hb, :], atno[:, hb, :],
                    start=(hb == 0), stop=(hb == NH - 1)), inc=MM if hb == NH - 1 else None)
            wait("v", MM)
            op("v", lambda v, d=d, s_i=s_i, pap=pap: v.tensor_add(
                R1[:, d, s_i * 512:(s_i + 1) * 512],
                R1[:, d, s_i * 512:(s_i + 1) * 512], pap), inc=VEC)
        barrier()

    if "acc" in dbg:
        for d in range(NDT):
            op("g", lambda g, d=d: g.dma_start(
                out=dbg_aps["dbg_acc"][d * 128:(d + 1) * 128, :], in_=R1[:, d, :]), inc=ST, k=16)

    if upto == "attn":
        _finish(nc, ENG, ST)
        return nc, dbg_aps

    r1_tiles = [R1[:, d, :] for d in range(NDT)]
    xh_tiles = [XH[:, d, :] for d in range(NDT)]
    ln_stats(r1_tiles, xh_tiles, 2048, mean_rep, rstd_rep, var_tmp)
    ln_apply(r1_tiles, xh_tiles, 2048, mean_rep, rstd_rep, var_tmp)
    barrier()
    ffn_pass(2048)
    barrier()

    for d in range(NDT):
        op("s", lambda s, d=d: s.dma_start(
            out=outT_d[d * 128:(d + 1) * 128, :], in_=R1[:, d, :]), inc=WS2, k=16)
    wait("sy", WS2)
    for tk in range(16):
        pp = tk % 2
        if tk >= 2:
            wait("sy", ST, ST.n - 16)
        op("sy", lambda sy, tk=tk, pp=pp: sy.dma_start_transpose(
            XH[:, pp, :], outT_d[:, tk * 128:(tk + 1) * 128]), inc=WS, k=16)
        wait("g", WS)
        op("g", lambda g, tk=tk, pp=pp: g.dma_start(
            out=out[tk * 128:(tk + 1) * 128, :], in_=XH[:, pp, :]), inc=ST, k=16)

    _finish(nc, ENG, ST)
    return nc, dbg_aps


def _finish(nc, ENG, ST):
    if ST.n > 0:
        ENG["g"].append(lambda e, v=ST.n, s=ST.sem: e.wait_ge(s, v))
    with nc.Block() as block:
        @block.gpsimd
        def _(g):
            for f in ENG["g"]:
                f(g)

        @block.sync
        def _(sy):
            for f in ENG["sy"]:
                f(sy)

        @block.tensor
        def _(t):
            for f in ENG["t"]:
                f(t)

        @block.scalar
        def _(s):
            for f in ENG["s"]:
                f(s)

        @block.vector
        def _(v):
            for f in ENG["v"]:
                f(v)


def _core_tables(even):
    chunks = EVEN_CHUNKS if even else ODD_CHUNKS
    rank = 0 if even else 1
    p = np.arange(128, dtype=np.int64)
    gki = np.zeros((128, NSLOT * NKV), np.int32)
    gvi = np.zeros((128, NSLOT * 4), np.int32)
    ebias = np.zeros((128, NSLOT * 8), np.float32)
    for s, g in enumerate(chunks):
        for kvh in range(NKV):
            gki[:, s * NKV + kvh] = (rank * 512 + kvh * 128 + p) * NSLOT + s
        for j in range(4):
            gvi[:, s * 4 + j] = rank * OWN + s * 512 + j * 128 + p
        U = UPROF[s]
        for u in range(8):
            if u == U - 1:
                b = EXP_SHIFT
            elif u < U - 1:
                b = EXP_SHIFT if u < g else WASTE_BIAS
            else:
                b = WASTE_BIAS
            ebias[:, s * 8 + u] = b
    return gki, gvi, ebias


def make_in_maps(L0_attn_x, L0_attn_wq, L0_attn_wk, L0_attn_wv, L0_attn_wo,
                 L0_ffn_w1, L0_ffn_w2, L0_ffn_w3):
    x = np.ascontiguousarray(np.asarray(L0_attn_x, dtype=np.float32))
    ws = {k: np.ascontiguousarray(np.asarray(v, dtype=np.float32)) for k, v in
          dict(w1=L0_ffn_w1, w2=L0_ffn_w2, w3=L0_ffn_w3, wq=L0_attn_wq,
               wk=L0_attn_wk, wv=L0_attn_wv, wo=L0_attn_wo).items()}
    in_maps = []
    for c in range(8):
        b = c // 2
        chunks = EVEN_CHUNKS if c % 2 == 0 else ODD_CHUNKS
        gki_a, gvi_a, eb = _core_tables(c % 2 == 0)
        xin = np.ascontiguousarray(
            np.concatenate([x[b, g * 512:(g + 1) * 512, :] for g in chunks], axis=0))
        m = dict(xin=xin, gki=gki_a, gvi=gvi_a, ebias=eb)
        m.update(ws)
        in_maps.append(m)
    return in_maps


def assemble(results):
    full = np.zeros((4, T, DM), np.float32)
    for c in range(8):
        b = c // 2
        chunks = EVEN_CHUNKS if c % 2 == 0 else ODD_CHUNKS
        o = np.asarray(results[c]["out"])
        for s, g in enumerate(chunks):
            full[b, g * 512:(g + 1) * 512, :] = o[s * 512:(s + 1) * 512, :]
    return full


def kernel(L0_attn_x, L0_attn_wq, L0_attn_wk, L0_attn_wv, L0_attn_wo,
           L0_ffn_w1, L0_ffn_w2, L0_ffn_w3):
    in_maps = make_in_maps(L0_attn_x, L0_attn_wq, L0_attn_wk, L0_attn_wv,
                           L0_attn_wo, L0_ffn_w1, L0_ffn_w2, L0_ffn_w3)
    nc, _ = build_program(upto="full")
    res = run_bass_kernel_spmd(nc, in_maps, core_ids=list(range(8)))
    return assemble(res.results)


# version 08 (final)
# speedup vs baseline: 1.0546x; 1.0546x over previous
import numpy as np
import concourse.bass as bass
import concourse.mybir as mybir
from concourse.bass_utils import run_bass_kernel_spmd
from concourse.bass_core_rust import IndirectOffsetOnAxis

F32 = mybir.dt.float32
BF16 = mybir.dt.bfloat16
I32 = mybir.dt.int32
AF = mybir.ActivationFunctionType
ALU = mybir.AluOpType

DM = 2048
NDT = DM // 128
T = 4096
HF = 5632
NHT = HF // 128
HD = 128
NH = 16
NKV = 4
CH = 512
NCH = T // CH
NSLOT = 4
UPROF = [2, 4, 6, 8]
STRIPE = 2048
NSTRIPE = T // STRIPE
HC = 4
NHCH = NHT // HC
NWEL = HF * DM
EXP_SHIFT = -24.0
WASTE_BIAS = -100.0
MASKVAL = -10000.0
SCALE = float(np.float32(1.0) / np.sqrt(np.float32(HD)))
EVEN_CHUNKS = [0, 3, 4, 7]
ODD_CHUNKS = [1, 2, 5, 6]
OWN = NSLOT * CH
PAIRS = [[0, 1], [2, 3], [4, 5], [6, 7]]
CMAP = {}
for _s, _g in enumerate(EVEN_CHUNKS):
    CMAP[_g] = (0, _s)
for _s, _g in enumerate(ODD_CHUNKS):
    CMAP[_g] = (1, _s)


class Ch:
    def __init__(self, sem):
        self.sem = sem
        self.n = 0


def build_program(upto="full", dbg=()):
    nc = bass.Bass("TRN2", target_bir_lowering=False)
    dbg = set(dbg)

    xin = nc.dram_tensor("xin", [OWN, DM], F32, kind="ExternalInput").ap()
    w1 = nc.dram_tensor("w1", [HF, DM], F32, kind="ExternalInput").ap()
    w2 = nc.dram_tensor("w2", [HF, DM], F32, kind="ExternalInput").ap()
    w3 = nc.dram_tensor("w3", [DM, HF], F32, kind="ExternalInput").ap()
    wq = nc.dram_tensor("wq", [DM, DM], F32, kind="ExternalInput").ap()
    wk = nc.dram_tensor("wk", [DM, 512], F32, kind="ExternalInput").ap()
    wv = nc.dram_tensor("wv", [DM, 512], F32, kind="ExternalInput").ap()
    wo = nc.dram_tensor("wo", [DM, DM], F32, kind="ExternalInput").ap()
    gki = nc.dram_tensor("gki", [128, NSLOT * NKV], I32, kind="ExternalInput").ap()
    gvi = nc.dram_tensor("gvi", [128, NSLOT * 4], I32, kind="ExternalInput").ap()
    ebias = nc.dram_tensor("ebias", [128, NSLOT * 8], F32, kind="ExternalInput").ap()
    out = nc.dram_tensor("out", [2048, DM], F32, kind="ExternalOutput").ap()

    w1q_d = nc.dram_tensor("w1q_d", [HF, DM], BF16, kind="Internal").ap()
    w2q_d = nc.dram_tensor("w2q_d", [HF, DM], BF16, kind="Internal").ap()
    w3q_d = nc.dram_tensor("w3q_d", [DM, HF], BF16, kind="Internal").ap()
    w1T_d = nc.dram_tensor("w1T_d", [DM, HF], BF16, kind="Internal").ap()
    w2T_d = nc.dram_tensor("w2T_d", [DM, HF], BF16, kind="Internal").ap()
    w3T_d = nc.dram_tensor("w3T_d", [HF, DM], BF16, kind="Internal").ap()
    wqb_d = nc.dram_tensor("wqb_d", [DM, DM], BF16, kind="Internal").ap()
    wkb_d = nc.dram_tensor("wkb_d", [DM, 512], BF16, kind="Internal").ap()
    wvb_d = nc.dram_tensor("wvb_d", [DM, 512], BF16, kind="Internal").ap()
    wob_d = nc.dram_tensor("wob_d", [DM, DM], BF16, kind="Internal").ap()
    xbf_d = nc.dram_tensor("xbf_d", [OWN, DM], BF16, kind="Internal").ap()
    kk_part = nc.dram_tensor("kk_part", [NKV * 128, OWN], BF16, kind="Internal").ap()
    vv_part = nc.dram_tensor("vv_part", [OWN, 512], BF16, kind="Internal").ap()
    kk_all = nc.dram_tensor("kk_all", [2 * NKV * 128, OWN], BF16, kind="Internal").ap()
    vv_all = nc.dram_tensor("vv_all", [2 * OWN, 512], BF16, kind="Internal").ap()
    outT_d = nc.dram_tensor("outT_d", [DM, 2048], BF16, kind="Internal").ap()

    dbg_aps = {}
    if "gamma" in dbg:
        dbg_aps["dbg_gamma"] = nc.dram_tensor("dbg_gamma", [1, 8], F32, kind="ExternalOutput").ap()
    if "w1q" in dbg:
        dbg_aps["dbg_w1q"] = nc.dram_tensor("dbg_w1q", [HF, DM], BF16, kind="ExternalOutput").ap()
        dbg_aps["dbg_w2q"] = nc.dram_tensor("dbg_w2q", [HF, DM], BF16, kind="ExternalOutput").ap()
        dbg_aps["dbg_w3q"] = nc.dram_tensor("dbg_w3q", [DM, HF], BF16, kind="ExternalOutput").ap()
        dbg_aps["dbg_w1T"] = nc.dram_tensor("dbg_w1T", [DM, HF], BF16, kind="ExternalOutput").ap()
        dbg_aps["dbg_w2T"] = nc.dram_tensor("dbg_w2T", [DM, HF], BF16, kind="ExternalOutput").ap()
        dbg_aps["dbg_w3T"] = nc.dram_tensor("dbg_w3T", [HF, DM], BF16, kind="ExternalOutput").ap()
    if "kk" in dbg:
        dbg_aps["dbg_kk"] = nc.dram_tensor("dbg_kk", [2 * NKV * 128, OWN], BF16, kind="ExternalOutput").ap()
        dbg_aps["dbg_vv"] = nc.dram_tensor("dbg_vv", [2 * OWN, 512], BF16, kind="ExternalOutput").ap()
    if "acc" in dbg:
        dbg_aps["dbg_acc"] = nc.dram_tensor("dbg_acc", [DM, 2048], BF16, kind="ExternalOutput").ap()
    if "r1" in dbg:
        dbg_aps["dbg_r1"] = nc.dram_tensor("dbg_r1", [DM, 2048], BF16, kind="ExternalOutput").ap()

    arena = nc.alloc_sbuf_tensor("ARENA", [128, 200 * 1024], mybir.dt.uint8)
    ARENA_BASE = nc.lookup_mloc(arena).addr
    OFF_R1 = ARENA_BASE + 0
    OFF_R2 = ARENA_BASE + 64 * 1024
    OFF_R3 = ARENA_BASE + 128 * 1024
    OFF_P = ARENA_BASE + 192 * 1024

    def sb(name, shape, dtype, off):
        return nc.alloc_sbuf_tensor_at(name, shape, dtype, offset=off).ap()

    R1 = sb("R1", [128, NDT, 2048], BF16, OFF_R1)
    XH = sb("XH", [128, NDT, 2048], BF16, OFF_R2)
    x2o = sb("x2o", [128, NDT, 512], BF16, OFF_R2)
    atno = sb("atno", [128, NH, 512], BF16, OFF_R2)
    qh = sb("qh", [128, NH, 512], BF16, OFF_R2 + 16 * 1024)
    wobuf = sb("wobuf", [128, 2, NH, 128], BF16, OFF_R2 + 16 * 1024)
    probs = sb("probs", [128, 2, 2, 512], BF16, OFF_R2 + 32 * 1024)
    recipR = sb("recipR", [128, 512], F32, OFF_R2 + 40 * 1024)
    sums2 = sb("sums2", [1, 2, 512], F32, OFF_R2 + 42 * 1024)
    recip2 = sb("recip2", [1, 2, 512], F32, OFF_R2 + 43 * 1024)
    mean_s = sb("mean_s", [128, 512], F32, OFF_R2 + 44 * 1024)
    rstd_s = sb("rstd_s", [128, 512], F32, OFF_R2 + 46 * 1024)
    var_s = sb("var_s", [128, 512], F32, OFF_R2 + 48 * 1024)
    wqbuf = sb("wqbuf", [128, 2, NDT, 128], BF16, OFF_R2 + 50 * 1024)
    hid = sb("hid", [128, HC, 2048], BF16, OFF_R3)
    w3buf = sb("w3buf", [128, HC, 2048], BF16, OFF_R3 + 16 * 1024)
    w1buf = sb("w1buf", [128, 2, NDT, 128], BF16, OFF_R3 + 32 * 1024)
    w2buf = sb("w2buf", [128, 2, NDT, 128], BF16, OFF_R3 + 40 * 1024)
    sgbuf = sb("sgbuf", [128, 2, 512], BF16, OFF_R3 + 48 * 1024)
    gwtile = sb("gwtile", [128, 3, 2048], F32, OFF_R3)
    qtile = sb("qtile", [128, 3, 2048], BF16, OFF_R3 + 24 * 1024)
    q2tile = sb("q2tile", [128, 2048], BF16, OFF_R3 + 36 * 1024)
    tbuf = sb("tbuf", [128, 2, 5632], BF16, OFF_R3 + 40 * 1024)
    kstag = sb("kstag", [128, NKV, 2048], BF16, OFF_R3)
    vstag = sb("vstag", [128, 16, 512], BF16, OFF_R3 + 16 * 1024)
    wkbuf = sb("wkbuf", [128, NDT, 512], BF16, OFF_R3 + 32 * 1024)
    wvbuf = sb("wvbuf", [128, NDT, 512], BF16, OFF_R3 + 48 * 1024)
    mean_rep = sb("mean_rep", [128, 2048], F32, OFF_R3)
    rstd_rep = sb("rstd_rep", [128, 2048], F32, OFF_R3 + 8 * 1024)
    var_tmp = sb("var_tmp", [128, 2048], F32, OFF_R3 + 16 * 1024)
    kch = sb("kch", [128, 7, NKV, 512], BF16, OFF_R3)
    vch = sb("vch", [128, 7, 4, NKV, 128], BF16, OFF_R3 + 28 * 1024)
    kdb = sb("kdb", [128, NKV, 512], BF16, OFF_R3 + 56 * 1024)
    vdb = sb("vdb", [128, 4, NKV, 128], BF16, OFF_R3 + 60 * 1024)
    M_diag = sb("M_diag", [128, 4, 512], BF16, OFF_P)
    ones128 = sb("ones128", [128, 128], BF16, OFF_P + 4 * 1024)
    onescol = sb("onescol", [128, 1], BF16, OFF_P + 4 * 1024 + 256)
    onesf_col = sb("onesf_col", [128, 1], F32, OFF_P + 4 * 1024 + 512)
    onesf_row = sb("onesf_row", [1, 128], F32, OFF_P + 4 * 1024 + 1024)
    gki_s = sb("gki_s", [128, NSLOT * NKV], I32, OFF_P + 6 * 1024 + 256)
    gvi_s = sb("gvi_s", [128, NSLOT * 4], I32, OFF_P + 6 * 1024 + 512)
    ebias_s = sb("ebias_s", [128, NSLOT * 8], F32, OFF_P + 6 * 1024 + 768)
    thr_s = sb("thr_s", [128, 8], F32, OFF_P + 7 * 1024)
    colsums = sb("colsums", [128, 3 * NHT], F32, OFF_P + 7 * 1024 + 64)
    gtmp = sb("gtmp", [1, 8], F32, OFF_P + 7 * 1024 + 640)
    gtmp2 = sb("gtmp2", [1, 8], F32, OFF_P + 7 * 1024 + 704)
    cvals = sb("cvals", [128, 4], F32, OFF_P + 7 * 1024 + 768)
    iotmp = sb("iotmp", [128, 4, 512], I32, OFF_R3 + 56 * 1024)
    tbw3 = sb("tbw3", [128, 2, 2048], BF16, OFF_R3 + 56 * 1024)

    psA = nc.alloc_psum_tensor("psA", [128, 2048], F32).ap()
    psB = nc.alloc_psum_tensor("psB", [128, 2048], F32).ap()

    GP = Ch(nc.alloc_semaphore(name="GP"))
    WS = Ch(nc.alloc_semaphore(name="WS"))
    WS2 = Ch(nc.alloc_semaphore(name="WS2"))
    MM = Ch(nc.alloc_semaphore(name="MM"))
    ACT = Ch(nc.alloc_semaphore(name="ACT"))
    VEC = Ch(nc.alloc_semaphore(name="VEC"))
    ST = Ch(nc.alloc_semaphore(name="ST"))
    AGS = Ch(nc.alloc_semaphore(name="AGS"))
    QL = [[Ch(nc.alloc_semaphore(name=f"QL{m}{p}")) for p in range(2)] for m in range(3)]
    QS = [Ch(nc.alloc_semaphore(name=f"QS{m}")) for m in range(3)]
    XBF = Ch(nc.alloc_semaphore(name="XBF"))
    XTP = Ch(nc.alloc_semaphore(name="XTP"))
    W3T = Ch(nc.alloc_semaphore(name="W3T"))
    W3TT = Ch(nc.alloc_semaphore(name="W3TT"))

    ENG = {"g": [], "sy": [], "t": [], "s": [], "v": []}

    def op(engine, fn, inc=None, k=1):
        if inc is not None:
            inc.n += k
            ENG[engine].append(lambda e, fn=fn, s=inc.sem, k=k: fn(e).then_inc(s, k))
            return inc.n
        ENG[engine].append(fn)
        return None

    def wait(engine, ch, val=None):
        v = ch.n if val is None else val
        if v > 0:
            ENG[engine].append(lambda e, s=ch.sem, v=v: e.wait_ge(s, v))

    def barrier():
        for e in ENG:
            for ch in (GP, WS, WS2, MM, ACT, VEC, AGS, XTP, W3T, W3TT):
                wait(e, ch)

    def ln_stats(src_tiles, sq_tiles, ncols, mean_ap, rstd_ap, var_ap):
        nnc = ncols // 512
        vsq = [0] * NDT
        for d in range(NDT):
            vsq[d] = op("v", lambda v, d=d: v.tensor_mul(sq_tiles[d], src_tiles[d],
                                                         src_tiles[d]), inc=VEC)
        for d in range(NDT):
            for j in range(nnc):
                op("t", lambda t, d=d, j=j: t.matmul(
                    psA[:, j * 512:(j + 1) * 512], ones128[:],
                    src_tiles[d][:, j * 512:(j + 1) * 512],
                    start=(d == 0), stop=(d == NDT - 1)), inc=None)
        for d in range(NDT):
            wait("t", VEC, vsq[d])
            for j in range(nnc):
                op("t", lambda t, d=d, j=j: t.matmul(
                    psB[:, j * 512:(j + 1) * 512], ones128[:],
                    sq_tiles[d][:, j * 512:(j + 1) * 512],
                    start=(d == 0), stop=(d == NDT - 1)),
                    inc=MM if (d == NDT - 1 and j == nnc - 1) else None)
        wait("v", MM)
        op("v", lambda v: v.tensor_scalar(mean_ap[:, 0:ncols], psA[:, 0:ncols],
                                          1.0 / DM, None, ALU.mult), inc=VEC)
        op("v", lambda v: v.tensor_scalar(var_ap[:, 0:ncols], psB[:, 0:ncols],
                                          1.0 / DM, None, ALU.mult), inc=VEC)
        op("v", lambda v: v.tensor_mul(rstd_ap[:, 0:ncols], mean_ap[:, 0:ncols],
                                       mean_ap[:, 0:ncols]), inc=VEC)
        op("v", lambda v: v.tensor_sub(var_ap[:, 0:ncols], var_ap[:, 0:ncols],
                                       rstd_ap[:, 0:ncols]), inc=VEC)
        wait("s", VEC)
        op("s", lambda s: s.activation(rstd_ap[:, 0:ncols], var_ap[:, 0:ncols],
                                       AF.Ln, bias=cvals[:, 0:1], scale=1.0), inc=ACT)
        op("s", lambda s: s.activation(rstd_ap[:, 0:ncols], rstd_ap[:, 0:ncols],
                                       AF.Exp, bias=0.0, scale=cvals[:, 1:2]), inc=ACT)
        wait("v", ACT)

    def ln_apply(src_tiles, dst_tiles, ncols, mean_ap, rstd_ap, tmp_ap):
        for d in range(NDT):
            op("v", lambda v, d=d: v.tensor_sub(tmp_ap[:, 0:ncols], src_tiles[d],
                                                mean_ap[:, 0:ncols]), inc=VEC)
            op("v", lambda v, d=d: v.tensor_mul(dst_tiles[d], tmp_ap[:, 0:ncols],
                                                rstd_ap[:, 0:ncols]), inc=VEC)

    def ffn_pass(ntok, first=False, w3t_gate=None, pre_gate=None, tp_hook=None):
        ntt = ntok // 512
        if first:
            wait("t", VEC)
        for hc_i in range(NHCH):
            wait("s", MM)
            if w3t_gate is not None:
                wait("s", W3T, w3t_gate[hc_i])
            for hti in range(HC):
                ht = hc_i * HC + hti
                op("s", lambda s, ht=ht, hti=hti: s.dma_start(
                    out=w3buf[:, hti, :],
                    in_=w3T_d[ht * 128:(ht + 1) * 128, :]), inc=WS2, k=16)
            for hti in range(HC):
                ht = hc_i * HC + hti
                pp = ht % 2
                if ht >= 2:
                    wait("sy", MM)
                if first and ht == 0:
                    wait("sy", WS2, pre_gate[0])
                    wait("sy", QS[2], pre_gate[1])
                op("sy", lambda sy, ht=ht, pp=pp: sy.dma_start(
                    out=w1buf[:, pp, :, :],
                    in_=w1T_d.rearrange("(dt p) H -> p dt H", p=128)[:, :, ht * 128:(ht + 1) * 128]),
                    inc=WS, k=16)
                op("sy", lambda sy, ht=ht, pp=pp: sy.dma_start(
                    out=w2buf[:, pp, :, :],
                    in_=w2T_d.rearrange("(dt p) H -> p dt H", p=128)[:, :, ht * 128:(ht + 1) * 128]),
                    inc=WS, k=16)
                wait("t", WS)
                for tt in range(ntt):
                    sl = tt % 2
                    x1ap = psA[:, sl * 512:(sl + 1) * 512]
                    x2ap = psA[:, 1024 + sl * 512:1024 + (sl + 1) * 512]
                    wait("t", VEC, VEC.n - 1)
                    for d in range(NDT):
                        op("t", lambda t, d=d, pp=pp, tt=tt, x1ap=x1ap: t.matmul(
                            x1ap, w1buf[:, pp, d, :], XH[:, d, tt * 512:(tt + 1) * 512],
                            start=(d == 0), stop=(d == NDT - 1)),
                            inc=MM if d == NDT - 1 else None)
                    for d in range(NDT):
                        op("t", lambda t, d=d, pp=pp, tt=tt, x2ap=x2ap: t.matmul(
                            x2ap, w2buf[:, pp, d, :], XH[:, d, tt * 512:(tt + 1) * 512],
                            start=(d == 0), stop=(d == NDT - 1)),
                            inc=MM if d == NDT - 1 else None)
                    wait("s", MM, MM.n - 1)
                    op("s", lambda s, sl=sl, x1ap=x1ap: s.activation(
                        sgbuf[:, sl, :], x1ap, AF.Sigmoid), inc=ACT)
                    wait("v", ACT)
                    wait("v", MM)
                    op("v", lambda v, sl=sl, hti=hti, tt=tt, x2ap=x2ap: v.tensor_mul(
                        hid[:, hti, tt * 512:(tt + 1) * 512], sgbuf[:, sl, :], x2ap), inc=VEC)
            if tp_hook is not None:
                tp_hook(hc_i + 1)
            wait("t", WS2)
            wait("t", VEC)
            for d in range(NDT):
                for tt in range(ntt):
                    sl = (d * ntt + tt) % 2
                    oap = psB[:, sl * 512:(sl + 1) * 512]
                    wait("t", VEC, VEC.n - 1)
                    for hti in range(HC):
                        op("t", lambda t, d=d, tt=tt, hti=hti, oap=oap: t.matmul(
                            oap, w3buf[:, hti, d * 128:(d + 1) * 128],
                            hid[:, hti, tt * 512:(tt + 1) * 512],
                            start=(hti == 0), stop=(hti == HC - 1)),
                            inc=MM if hti == HC - 1 else None)
                    wait("v", MM)
                    op("v", lambda v, d=d, tt=tt, oap=oap: v.tensor_add(
                        R1[:, d, tt * 512:(tt + 1) * 512],
                        R1[:, d, tt * 512:(tt + 1) * 512], oap), inc=VEC)

    op("g", lambda g: g.memset(ones128[:], 1.0))
    op("g", lambda g: g.memset(cvals[:, 0:1], 1e-5))
    op("g", lambda g: g.memset(cvals[:, 1:2], -0.5))
    op("g", lambda g: g.memset(cvals[:, 2:3], SCALE))
    op("g", lambda g: g.memset(onescol[:], 1.0))
    op("g", lambda g: g.memset(onesf_col[:], 1.0))
    op("g", lambda g: g.memset(onesf_row[:], 1.0))
    op("g", lambda g: g.iota(iotmp[:], pattern=[[128, 4], [-1, 512]], base=0,
                             channel_multiplier=1), inc=GP, k=1)
    op("g", lambda g: g.dma_start(out=gki_s[:], in_=gki[:]), inc=GP, k=16)
    op("g", lambda g: g.dma_start(out=gvi_s[:], in_=gvi[:]), inc=GP, k=16)
    op("g", lambda g: g.dma_start(out=ebias_s[:], in_=ebias[:]), inc=GP, k=16)
    op("g", lambda g: g.dma_start(out=xbf_d[:], in_=xin[:]), inc=XBF, k=16)
    op("g", lambda g: g.dma_start(out=wqb_d[:], in_=wq[:]), inc=GP, k=16)
    op("g", lambda g: g.dma_start(out=wkb_d[:], in_=wk[:]), inc=GP, k=16)
    op("g", lambda g: g.dma_start(out=wvb_d[:], in_=wv[:]), inc=GP, k=16)
    op("g", lambda g: g.dma_start(out=wob_d[:], in_=wo[:]), inc=GP, k=16)
    gp_wcast = GP.n
    wait("v", GP, 1)
    op("v", lambda v: v.tensor_scalar(M_diag[:], iotmp[:], 0.0, MASKVAL,
                                      ALU.is_gt, ALU.mult), inc=VEC)

    WMATS = [
        (w1, w1q_d, False, 0),
        (w2, w2q_d, False, 2),
        (w3, w3q_d, True, 4),
    ]
    QENG = [("sy", WS), ("s", WS2), ("g", GP)]
    NTILE = NWEL // (128 * 2048)
    flats = []
    for (wsrc, qdst, fold, ti) in WMATS:
        flats.append((
            wsrc.rearrange("a b -> (a b)").rearrange("(t p c) -> t p c", p=128, c=2048),
            qdst.rearrange("a b -> (a b)").rearrange("(t p c) -> t p c", p=128, c=2048)))
    red_cnt = [[0] * NTILE for _ in range(3)]
    load_cnt = [[0] * NTILE for _ in range(3)]
    for t_i in range(NTILE):
        for mi in range(3):
            eng, _ = QENG[mi]
            if t_i > 0:
                wait(eng, VEC, red_cnt[mi][t_i - 1])
            load_cnt[mi][t_i] = op(eng, lambda e, mi=mi, t_i=t_i:
                                   e.dma_start(out=gwtile[:, mi, :], in_=flats[mi][0][t_i]),
                                   inc=QL[mi][t_i % 2], k=16)
        for mi in range(3):
            wait("v", QL[mi][t_i % 2], load_cnt[mi][t_i])
            red_cnt[mi][t_i] = op("v", lambda v, mi=mi, t_i=t_i:
                                  v.tensor_reduce(colsums[:, mi * NTILE + t_i:mi * NTILE + t_i + 1],
                                                  gwtile[:, mi, :], mybir.AxisListType.X, ALU.add,
                                                  apply_absolute_value=True), inc=VEC)
    wait("s", XBF, 16)
    for d in range(NDT):
        op("s", lambda s, d=d: s.dma_start_transpose(
            R1[:, d, :], xbf_d[0:OWN, d * 128:(d + 1) * 128]), inc=XTP, k=16)
    for mi, (wsrc, qdst, fold, ti) in enumerate(WMATS):
        wait("t", VEC)
        op("t", lambda t, mi=mi: t.matmul(psA[0:1, mi * 64:mi * 64 + NTILE], onesf_col[:],
                                          colsums[:, mi * NTILE:(mi + 1) * NTILE],
                                          start=True, stop=True), inc=MM)
        wait("v", MM)
        op("v", lambda v, mi=mi, ti=ti: v.tensor_reduce(
            gtmp[0:1, ti:ti + 1], psA[0:1, mi * 64:mi * 64 + NTILE],
            mybir.AxisListType.X, ALU.add), inc=VEC)
        wait("v", VEC)
        op("v", lambda v, ti=ti: v.tensor_scalar(
            gtmp[0:1, ti:ti + 1], gtmp[0:1, ti:ti + 1],
            float(np.float32(1.0) / np.float32(NWEL)), None, ALU.mult), inc=VEC)
        wait("v", VEC)
        op("v", lambda v, ti=ti: v.tensor_scalar(
            gtmp2[0:1, ti:ti + 1], gtmp[0:1, ti:ti + 1],
            1e-8, 0.5, ALU.add, ALU.mult), inc=VEC)
        wait("v", VEC)
        op("v", lambda v, ti=ti: v.tensor_scalar(
            gtmp2[0:1, ti + 1:ti + 2], gtmp2[0:1, ti:ti + 1],
            -1.0, None, ALU.mult), inc=VEC)
        wait("t", VEC)
        wait("t", ACT)
        op("t", lambda t, mi=mi, ti=ti: t.matmul(psB[:, mi * 2:mi * 2 + 2], onesf_row[:],
                                                 gtmp2[0:1, ti:ti + 2], start=True, stop=True), inc=MM)
        wait("s", MM)
        op("s", lambda s, mi=mi, ti=ti: s.copy(thr_s[:, ti:ti + 2], psB[:, mi * 2:mi * 2 + 2]), inc=ACT)
    act_thr = ACT.n
    qnt_cnt = [[0] * NTILE for _ in range(3)]
    st_cnt = [[0] * NTILE for _ in range(3)]
    for t_i in range(NTILE):
        for mi in range(3):
            eng, _ = QENG[mi]
            wait(eng, VEC, (red_cnt[mi][NTILE - 1] if t_i == 0 else qnt_cnt[mi][t_i - 1]))
            load_cnt[mi][t_i] = op(eng, lambda e, mi=mi, t_i=t_i:
                                   e.dma_start(out=gwtile[:, mi, :], in_=flats[mi][0][t_i]),
                                   inc=QL[mi][t_i % 2], k=16)
        for mi, (wsrc, qdst, fold, ti) in enumerate(WMATS):
            phl = 0.5 if fold else 1.0
            wait("v", QL[mi][t_i % 2], load_cnt[mi][t_i])
            if t_i == 0:
                wait("v", ACT, act_thr)
            if t_i >= 1:
                wait("v", QS[mi], st_cnt[mi][t_i - 1])

            def quant(v, mi=mi, ti=ti, phl=phl):
                v.tensor_scalar(qtile[:, mi, :], gwtile[:, mi, :],
                                thr_s[:, ti:ti + 1], phl, ALU.is_ge, ALU.mult)
                v.tensor_scalar(q2tile[:], gwtile[:, mi, :],
                                thr_s[:, ti + 1:ti + 2], phl, ALU.is_le, ALU.mult)
                return v.tensor_sub(qtile[:, mi, :], qtile[:, mi, :], q2tile[:])
            qnt_cnt[mi][t_i] = op("v", quant, inc=VEC)
            wait("g", VEC, qnt_cnt[mi][t_i])
            st_cnt[mi][t_i] = op("g", lambda g, mi=mi, t_i=t_i:
                                 g.dma_start(out=flats[mi][1][t_i], in_=qtile[:, mi, :]),
                                 inc=QS[mi], k=16)

    r1_tiles = [R1[:, d, :] for d in range(NDT)]
    xh_tiles = [XH[:, d, :] for d in range(NDT)]
    wait("v", XTP)
    wait("t", XTP)
    wait("t", VEC)
    wait("t", ACT, act_thr)
    ln_stats(r1_tiles, xh_tiles, OWN, mean_rep, rstd_rep, var_tmp)
    ln_apply(r1_tiles, xh_tiles, OWN, mean_rep, rstd_rep, var_tmp)

    TMATS = [(w1q_d, w1T_d, HF, DM, 0), (w2q_d, w2T_d, HF, DM, 1)]
    for (src_d, dst_d, rows, cols, mi) in TMATS:
        wait("sy", QS[mi], st_cnt[mi][NTILE - 1])
        for d in range(cols // 128):
            pp = d % 2
            wait("sy", WS2, WS2.n if d < 2 else WS2.n - 16)
            op("sy", lambda sy, d=d, pp=pp, src_d=src_d, rows=rows:
               sy.dma_start_transpose(tbuf[:, pp, 0:rows],
                                      src_d[:, d * 128:(d + 1) * 128]), inc=WS, k=16)
            wait("s", WS)
            op("s", lambda s, d=d, pp=pp, dst_d=dst_d, rows=rows:
               s.dma_start(out=dst_d[d * 128:(d + 1) * 128, :],
                           in_=tbuf[:, pp, 0:rows]), inc=WS2, k=16)
    w1w2t_done = WS2.n
    w3q_done = st_cnt[2][NTILE - 1]
    w3t_st = [0] * NHT
    w3t_chunk = [0] * NHCH

    def emit_w3_chunk(hc):
        if hc >= NHCH:
            return
        for d in range(hc * HC, (hc + 1) * HC):
            if d == 0:
                wait("sy", QS[2], w3q_done)
                wait("sy", WS2, w1w2t_done)
            if d >= 2:
                wait("sy", W3T, w3t_st[d - 2])
            op("sy", lambda sy, d=d: sy.dma_start_transpose(
                tbw3[:, d % 2, :], w3q_d[:, d * 128:(d + 1) * 128]), inc=W3TT, k=16)
            wait("g", W3TT)
            w3t_st[d] = op("g", lambda g, d=d: g.dma_start(
                out=w3T_d[d * 128:(d + 1) * 128, :], in_=tbw3[:, d % 2, :]), inc=W3T, k=16)
        w3t_chunk[hc] = w3t_st[(hc + 1) * HC - 1]

    emit_w3_chunk(0)

    if "gamma" in dbg:
        wait("s", VEC)
        op("s", lambda s: s.dma_start(out=dbg_aps["dbg_gamma"][:], in_=gtmp[0:1, :]), inc=ST, k=16)
    if "w1q" in dbg:
        wait("g", WS2)
        wait("g", GP)
        wait("g", W3T)
        op("g", lambda g: g.dma_start(out=dbg_aps["dbg_w1q"][:], in_=w1q_d[:]), inc=ST, k=16)
        op("g", lambda g: g.dma_start(out=dbg_aps["dbg_w2q"][:], in_=w2q_d[:]), inc=ST, k=16)
        op("g", lambda g: g.dma_start(out=dbg_aps["dbg_w3q"][:], in_=w3q_d[:]), inc=ST, k=16)
        op("g", lambda g: g.dma_start(out=dbg_aps["dbg_w1T"][:], in_=w1T_d[:]), inc=ST, k=16)
        op("g", lambda g: g.dma_start(out=dbg_aps["dbg_w2T"][:], in_=w2T_d[:]), inc=ST, k=16)
        op("g", lambda g: g.dma_start(out=dbg_aps["dbg_w3T"][:], in_=w3T_d[:]), inc=ST, k=16)

    if upto == "prep":
        _finish(nc, ENG, ST)
        return nc, dbg_aps

    ffn_pass(OWN, first=True, w3t_gate=w3t_chunk, pre_gate=(w1w2t_done, w3q_done),
             tp_hook=emit_w3_chunk)
    barrier()
    if "r1" in dbg:
        for d in range(NDT):
            op("g", lambda g, d=d: g.dma_start(
                out=dbg_aps["dbg_r1"][d * 128:(d + 1) * 128, :], in_=R1[:, d, :]), inc=ST, k=16)
        wait("v", ST)
    ln_stats(r1_tiles, xh_tiles, OWN, mean_rep, rstd_rep, var_tmp)
    ln_apply(r1_tiles, xh_tiles, OWN, mean_rep, rstd_rep, var_tmp)
    barrier()
    wait("sy", GP, gp_wcast)
    op("sy", lambda sy: sy.dma_start(
        out=wkbuf[:], in_=wkb_d.rearrange("(dt p) f -> p dt f", p=128)), inc=WS, k=16)
    op("sy", lambda sy: sy.dma_start(
        out=wvbuf[:], in_=wvb_d.rearrange("(dt p) f -> p dt f", p=128)), inc=WS, k=16)
    wait("t", WS)
    for kvh in range(NKV):
        for tt in range(OWN // 512):
            sl = (kvh * 4 + tt) % 2
            pap = psA[:, sl * 512:(sl + 1) * 512]
            wait("t", ACT, ACT.n - 1)
            for d in range(NDT):
                op("t", lambda t, d=d, kvh=kvh, tt=tt, pap=pap: t.matmul(
                    pap, wkbuf[:, d, kvh * 128:(kvh + 1) * 128],
                    XH[:, d, tt * 512:(tt + 1) * 512],
                    start=(d == 0), stop=(d == NDT - 1)), inc=MM if d == NDT - 1 else None)
            wait("s", MM)
            op("s", lambda s, kvh=kvh, tt=tt, pap=pap: s.copy(
                kstag[:, kvh, tt * 512:(tt + 1) * 512], pap), inc=ACT)
    wait("g", ACT)
    op("g", lambda g: g.dma_start(
        out=kk_part.rearrange("(k p) t -> p k t", p=128), in_=kstag[:]), inc=GP, k=16)
    wait("g", GP)
    op("g", lambda g: g.collective_compute(
        "AllGather", ALU.bypass, replica_groups=PAIRS,
        ins=[kk_part.opt()], outs=[kk_all.opt()]), inc=AGS, k=1)
    for tk in range(OWN // 128):
        sl = tk % 2
        pap = psB[:, sl * 512:(sl + 1) * 512]
        wait("t", ACT, ACT.n - 1)
        for d in range(NDT):
            op("t", lambda t, d=d, tk=tk, pap=pap: t.matmul(
                pap, XH[:, d, tk * 128:(tk + 1) * 128], wvbuf[:, d, :],
                start=(d == 0), stop=(d == NDT - 1)), inc=MM if d == NDT - 1 else None)
        wait("s", MM)
        op("s", lambda s, tk=tk, pap=pap: s.copy(vstag[:, tk, :], pap), inc=ACT)
    wait("g", ACT)
    op("g", lambda g: g.dma_start(
        out=vv_part.rearrange("(tk p) f -> p tk f", p=128), in_=vstag[:]), inc=GP, k=16)
    wait("g", GP)
    op("g", lambda g: g.collective_compute(
        "AllGather", ALU.bypass, replica_groups=PAIRS,
        ins=[vv_part.opt()], outs=[vv_all.opt()]), inc=AGS, k=1)

    if "kk" in dbg:
        wait("g", AGS)
        op("g", lambda g: g.dma_start(out=dbg_aps["dbg_kk"][:], in_=kk_all[:]), inc=ST, k=16)
        op("g", lambda g: g.dma_start(out=dbg_aps["dbg_vv"][:], in_=vv_all[:]), inc=ST, k=16)

    if upto == "p1":
        _finish(nc, ENG, ST)
        return nc, dbg_aps

    kk_rows = kk_all.rearrange("r (s t) -> (r s) t", t=512)
    loaded_chunks = [0]

    def load_chunks(upper):
        for c in range(loaded_chunks[0], upper):
            cr, cp = CMAP[c]
            op("sy", lambda sy, c=c, cr=cr, cp=cp: sy.dma_start(
                out=kch[:, c, :, :],
                in_=kk_all[cr * 512:(cr + 1) * 512,
                           cp * 512:(cp + 1) * 512].rearrange("(k p) t -> p k t", p=128)),
                inc=WS, k=16)
            op("sy", lambda sy, c=c, cr=cr, cp=cp: sy.dma_start(
                out=vch[:, c, :, :, :],
                in_=vv_all[cr * OWN + cp * 512:cr * OWN + (cp + 1) * 512,
                           :].rearrange("(j p) (k d) -> p j k d", p=128, d=128)),
                inc=WS, k=16)
        loaded_chunks[0] = max(loaded_chunks[0], upper)

    for s_i in range(NSLOT):
        U = UPROF[s_i]
        if s_i == 0:
            wait("g", AGS)
            wait("sy", AGS)
        for kvh in range(NKV):
            op("g", lambda g, kvh=kvh, s_i=s_i: g.indirect_dma_start(
                kdb[:, kvh, :], None, kk_rows,
                IndirectOffsetOnAxis(gki_s[:, s_i * NKV + kvh:s_i * NKV + kvh + 1], 0)),
                inc=GP, k=16)
        for j in range(4):
            op("g", lambda g, j=j, s_i=s_i: g.indirect_dma_start(
                vdb[:, j, :, :], None, vv_all,
                IndirectOffsetOnAxis(gvi_s[:, s_i * 4 + j:s_i * 4 + j + 1], 0)),
                inc=GP, k=16)
        load_chunks(U - 1)
        wait("t", GP)
        wait("v", GP)
        sl_tiles = [R1[:, d, s_i * 512:(s_i + 1) * 512] for d in range(NDT)]
        xo_tiles = [x2o[:, d, :] for d in range(NDT)]
        ln_stats(sl_tiles, xo_tiles, 512, mean_s, rstd_s, var_s)
        ln_apply(sl_tiles, xo_tiles, 512, mean_s, rstd_s, var_s)
        barrier()
        for hh in range(NH):
            pp = hh % 2
            if hh >= 2:
                wait("sy", MM)
            op("sy", lambda sy, hh=hh, pp=pp: sy.dma_start(
                out=wqbuf[:, pp, :, :],
                in_=wqb_d.rearrange("(dt p) f -> p dt f", p=128)[:, :, hh * 128:(hh + 1) * 128]),
                inc=WS, k=16)
            wait("t", WS)
            pap = psB[:, (hh % 2) * 512:((hh % 2) + 1) * 512]
            wait("t", ACT, ACT.n - 1)
            for d in range(NDT):
                op("t", lambda t, d=d, pp=pp, pap=pap: t.matmul(
                    pap, wqbuf[:, pp, d, :], x2o[:, d, :],
                    start=(d == 0), stop=(d == NDT - 1)), inc=MM if d == NDT - 1 else None)
            wait("s", MM)
            op("s", lambda s, hh=hh, pap=pap: s.copy(qh[:, hh, :], pap), inc=ACT)
        barrier()
        if s_i + 1 < NSLOT:
            load_chunks(UPROF[s_i + 1] - 1)
        act_evac = {}
        vec_mul = {-1: VEC.n, 0: VEC.n}
        sums_copy = {}
        recip_n = {}
        pend = None
        for h in range(NH):
            kvh = h // NKV
            osl = h % 2
            psO = psB[:, osl * 512:(osl + 1) * 512]
            psS = psB[0:1, 1024 + osl * 512:1024 + osl * 512 + 512]
            if h >= 2:
                wait("t", ACT, act_evac[h - 2])
                wait("t", VEC, sums_copy[h - 2])
            if h >= 1:
                wait("t", VEC, vec_mul.get(h - 2, 0))
            sus = [(u, su) for u in range(U) for su in range(2)]
            nsu = len(sus)

            def emit_qk(i, lag=1):
                u, su = sus[i]
                kbuf = kdb if u == U - 1 else kch[:, u, :, :]
                scA = psA[:, (i % 2) * 1024:(i % 2) * 1024 + 1024]
                wait("t", ACT, ACT.n - lag)
                for jj in range(2):
                    jg = su * 2 + jj
                    op("t", lambda t, jg=jg, jj=jj, kbuf=kbuf, kvh=kvh, h=h, scA=scA: t.matmul(
                        scA[:, jj * 512:(jj + 1) * 512],
                        kbuf[:, kvh, jg * 128:(jg + 1) * 128], qh[:, h, :],
                        start=True, stop=True), inc=MM if jj == 1 else None)

            emit_qk(0)
            for i in range(nsu):
                u, su = sus[i]
                is_diag = (u == U - 1)
                scA = psA[:, (i % 2) * 1024:(i % 2) * 1024 + 1024]
                pb = i % 2
                if is_diag:
                    wait("v", MM)
                    op("v", lambda v, su=su, scA=scA: v.scalar_tensor_tensor(
                        scA[:],
                        M_diag[:, su * 2:(su + 1) * 2, :].rearrange("p a b -> p (a b)"),
                        1.0, scA[:], ALU.mult, ALU.add), inc=VEC)
                    wait("s", VEC)
                else:
                    wait("s", MM)
                for ej in range(2):
                    op("s", lambda s, pb=pb, s_i=s_i, u=u, ej=ej, scA=scA: s.activation(
                        probs[:, pb, ej, :], scA[:, ej * 512:(ej + 1) * 512],
                        AF.Exp, bias=ebias_s[:, s_i * 8 + u:s_i * 8 + u + 1],
                        scale=cvals[:, 2:3]), inc=ACT)
                if i + 1 < nsu:
                    emit_qk(i + 1, lag=2)
                if i == 1 and pend is not None:
                    hp = pend
                    bb = psB[:, (hp % 2) * 512:((hp % 2) + 1) * 512]
                    wait("t", ACT, act_evac[hp])
                    wait("t", VEC, recip_n[hp])
                    op("t", lambda t, hp=hp, bb=bb: t.matmul(
                        bb, onesf_row[:], recip2[0:1, hp % 2, :],
                        start=True, stop=True), inc=MM)
                    wait("v", MM)
                    vec_mul[hp] = op("v", lambda v, hp=hp, bb=bb: v.tensor_mul(
                        atno[:, hp, :], bb, atno[:, hp, :]), inc=VEC)
                    pend = None
                vbuf = vdb if is_diag else vch[:, u, :, :, :]
                last = (i == nsu - 1)
                for jj in range(2):
                    if jj == 0:
                        wait("t", ACT, ACT.n - 1)
                    else:
                        wait("t", ACT)
                    jg = su * 2 + jj
                    op("t", lambda t, jj=jj, pb=pb, psS=psS,
                       first=(i == 0 and jj == 0), stp=(last and jj == 1): t.matmul(
                        psS, onescol[:], probs[:, pb, jj, :],
                        start=first, stop=stp), inc=None)
                    op("t", lambda t, jj=jj, jg=jg, pb=pb, vbuf=vbuf, kvh=kvh,
                       first=(i == 0 and jj == 0), stp=(last and jj == 1), psO=psO: t.matmul(
                        psO, vbuf[:, jg, kvh, :], probs[:, pb, jj, :],
                        start=first, stop=stp),
                        inc=MM if jj == 1 else None)
            wait("s", MM)
            act_evac[h] = op("s", lambda s, h=h, psO=psO: s.copy(atno[:, h, :], psO), inc=ACT)
            wait("v", MM)
            sums_copy[h] = op("v", lambda v, osl=osl, psS=psS: v.tensor_copy(
                sums2[0:1, osl, :], psS), inc=VEC)
            wait("v", VEC)
            recip_n[h] = op("v", lambda v, osl=osl: v.reciprocal(
                recip2[0:1, osl, :], sums2[0:1, osl, :]), inc=VEC)
            pend = h
        hp = pend
        bb = psB[:, (hp % 2) * 512:((hp % 2) + 1) * 512]
        wait("t", ACT, act_evac[hp])
        wait("t", VEC, recip_n[hp])
        op("t", lambda t, hp=hp, bb=bb: t.matmul(
            bb, onesf_row[:], recip2[0:1, hp % 2, :], start=True, stop=True), inc=MM)
        wait("v", MM)
        op("v", lambda v, hp=hp, bb=bb: v.tensor_mul(
            atno[:, hp, :], bb, atno[:, hp, :]), inc=VEC)
        barrier()
        for d in range(NDT):
            pp = d % 2
            if d >= 2:
                wait("sy", MM)
            op("sy", lambda sy, d=d, pp=pp: sy.dma_start(
                out=wobuf[:, pp, :, :],
                in_=wob_d.rearrange("(hb p) f -> p hb f", p=128)[:, :, d * 128:(d + 1) * 128]),
                inc=WS, k=16)
            wait("t", WS)
            pap = psA[:, (d % 2) * 512:((d % 2) + 1) * 512]
            wait("t", VEC, VEC.n - 1)
            for hb in range(NH):
                op("t", lambda t, hb=hb, pp=pp, pap=pap: t.matmul(
                    pap, wobuf[:, pp, hb, :], atno[:, hb, :],
                    start=(hb == 0), stop=(hb == NH - 1)), inc=MM if hb == NH - 1 else None)
            wait("v", MM)
            op("v", lambda v, d=d, s_i=s_i, pap=pap: v.tensor_add(
                R1[:, d, s_i * 512:(s_i + 1) * 512],
                R1[:, d, s_i * 512:(s_i + 1) * 512], pap), inc=VEC)
        barrier()

    if "acc" in dbg:
        for d in range(NDT):
            op("g", lambda g, d=d: g.dma_start(
                out=dbg_aps["dbg_acc"][d * 128:(d + 1) * 128, :], in_=R1[:, d, :]), inc=ST, k=16)

    if upto == "attn":
        _finish(nc, ENG, ST)
        return nc, dbg_aps

    r1_tiles = [R1[:, d, :] for d in range(NDT)]
    xh_tiles = [XH[:, d, :] for d in range(NDT)]
    ln_stats(r1_tiles, xh_tiles, 2048, mean_rep, rstd_rep, var_tmp)
    ln_apply(r1_tiles, xh_tiles, 2048, mean_rep, rstd_rep, var_tmp)
    barrier()
    ffn_pass(2048)
    barrier()

    for d in range(NDT):
        op("s", lambda s, d=d: s.dma_start(
            out=outT_d[d * 128:(d + 1) * 128, :], in_=R1[:, d, :]), inc=WS2, k=16)
    wait("sy", WS2)
    for tk in range(16):
        pp = tk % 2
        if tk >= 2:
            wait("sy", ST, ST.n - 16)
        op("sy", lambda sy, tk=tk, pp=pp: sy.dma_start_transpose(
            XH[:, pp, :], outT_d[:, tk * 128:(tk + 1) * 128]), inc=WS, k=16)
        wait("g", WS)
        op("g", lambda g, tk=tk, pp=pp: g.dma_start(
            out=out[tk * 128:(tk + 1) * 128, :], in_=XH[:, pp, :]), inc=ST, k=16)

    _finish(nc, ENG, ST)
    return nc, dbg_aps


def _finish(nc, ENG, ST):
    if ST.n > 0:
        ENG["g"].append(lambda e, v=ST.n, s=ST.sem: e.wait_ge(s, v))
    with nc.Block() as block:
        @block.gpsimd
        def _(g):
            for f in ENG["g"]:
                f(g)

        @block.sync
        def _(sy):
            for f in ENG["sy"]:
                f(sy)

        @block.tensor
        def _(t):
            for f in ENG["t"]:
                f(t)

        @block.scalar
        def _(s):
            for f in ENG["s"]:
                f(s)

        @block.vector
        def _(v):
            for f in ENG["v"]:
                f(v)


def _core_tables(even):
    chunks = EVEN_CHUNKS if even else ODD_CHUNKS
    rank = 0 if even else 1
    p = np.arange(128, dtype=np.int64)
    gki = np.zeros((128, NSLOT * NKV), np.int32)
    gvi = np.zeros((128, NSLOT * 4), np.int32)
    ebias = np.zeros((128, NSLOT * 8), np.float32)
    for s, g in enumerate(chunks):
        for kvh in range(NKV):
            gki[:, s * NKV + kvh] = (rank * 512 + kvh * 128 + p) * NSLOT + s
        for j in range(4):
            gvi[:, s * 4 + j] = rank * OWN + s * 512 + j * 128 + p
        U = UPROF[s]
        for u in range(8):
            if u == U - 1:
                b = EXP_SHIFT
            elif u < U - 1:
                b = EXP_SHIFT if u < g else WASTE_BIAS
            else:
                b = WASTE_BIAS
            ebias[:, s * 8 + u] = b
    return gki, gvi, ebias


def make_in_maps(L0_attn_x, L0_attn_wq, L0_attn_wk, L0_attn_wv, L0_attn_wo,
                 L0_ffn_w1, L0_ffn_w2, L0_ffn_w3):
    x = np.ascontiguousarray(np.asarray(L0_attn_x, dtype=np.float32))
    ws = {k: np.ascontiguousarray(np.asarray(v, dtype=np.float32)) for k, v in
          dict(w1=L0_ffn_w1, w2=L0_ffn_w2, w3=L0_ffn_w3, wq=L0_attn_wq,
               wk=L0_attn_wk, wv=L0_attn_wv, wo=L0_attn_wo).items()}
    in_maps = []
    for c in range(8):
        b = c // 2
        chunks = EVEN_CHUNKS if c % 2 == 0 else ODD_CHUNKS
        gki_a, gvi_a, eb = _core_tables(c % 2 == 0)
        xin = np.ascontiguousarray(
            np.concatenate([x[b, g * 512:(g + 1) * 512, :] for g in chunks], axis=0))
        m = dict(xin=xin, gki=gki_a, gvi=gvi_a, ebias=eb)
        m.update(ws)
        in_maps.append(m)
    return in_maps


def assemble(results):
    full = np.zeros((4, T, DM), np.float32)
    for c in range(8):
        b = c // 2
        chunks = EVEN_CHUNKS if c % 2 == 0 else ODD_CHUNKS
        o = np.asarray(results[c]["out"])
        for s, g in enumerate(chunks):
            full[b, g * 512:(g + 1) * 512, :] = o[s * 512:(s + 1) * 512, :]
    return full


def kernel(L0_attn_x, L0_attn_wq, L0_attn_wk, L0_attn_wv, L0_attn_wo,
           L0_ffn_w1, L0_ffn_w2, L0_ffn_w3):
    in_maps = make_in_maps(L0_attn_x, L0_attn_wq, L0_attn_wk, L0_attn_wv,
                           L0_attn_wo, L0_ffn_w1, L0_ffn_w2, L0_ffn_w3)
    nc, _ = build_program(upto="full")
    res = run_bass_kernel_spmd(nc, in_maps, core_ids=list(range(8)))
    return assemble(res.results)
